# version 08
# speedup vs baseline: 1.5865x; 1.0468x over previous
import contextlib
import os
import numpy as np
import ml_dtypes
import concourse.bass as bass
import concourse.mybir as mybir
from concourse.bass_utils import run_bass_kernel_spmd

F32 = mybir.dt.float32
BF16 = mybir.dt.bfloat16
AF = mybir.ActivationFunctionType
ALU = mybir.AluOpType
AX = mybir.AxisListType.X

B, T, D = 8, 2048, 1024
HN = 8
NPT = 8
NSEG = 4
TS = T // NSEG
CH = 64
NCHS = TS // CH
NU = 4 * NCHS
SBLK = 256
DFF = 4096
DC = 256
S16 = 0.0625
ENGS = ("sync", "tensor", "vector", "scalar", "gpsimd")
TP2 = [(0, 0), (64, 64)]
ALL = ENGS


def ts_(i, sz):
    return slice(i * sz, (i + 1) * sz)


class _Rec:

    def __init__(self, eng):
        object.__setattr__(self, "_e", eng)
        object.__setattr__(self, "last", None)

    def __getattr__(self, n):
        a = getattr(self._e, n)
        if callable(a):
            def w(*args, **kw):
                r = a(*args, **kw)
                if hasattr(r, "then_inc"):
                    object.__setattr__(self, "last", r)
                return r
            return w
        return a


class K:
    def __init__(self, nc, block, es):
        self.nc, self.block, self.es = nc, block, es
        self.tick = {e: [es.enter_context(nc.semaphore("tk_" + e)), 0] for e in ENGS}
        self.dma_sems = {}
        self.consts = {}
        self.want_dbg = set()
        self.dbgmap = {}
        self.n_steps = 0

    def dsem(self, name):
        if name not in self.dma_sems:
            self.dma_sems[name] = [self.es.enter_context(self.nc.semaphore(name)), 0]
        return self.dma_sems[name]

    def step(self, engine, deps=(), body=None, dma=(), deps_at=()):
        snap = [(self.tick[e][0], self.tick[e][1]) for e in deps
                if e != engine and self.tick[e][1] > 0]
        snap += [(self.tick[e][0], v) for (e, v) in deps_at if v > 0]
        dsnap = [(d[0], n) for (d, n) in dma if n > 0]
        tk = self.tick[engine]
        self.n_steps += 1

        @getattr(self.block, engine)
        def _(eng):
            for h, n in snap:
                eng.wait_ge(h, n)
            for h, n in dsnap:
                eng.wait_ge(h, n)
            rec = _Rec(eng)
            body(rec)
            done = False
            if rec.last is not None:
                try:
                    rec.last.then_inc(tk[0], 1)
                    done = True
                except Exception:
                    done = False
            if not done:
                if engine in ("vector", "scalar", "gpsimd"):
                    eng.drain()
                eng.sem_inc(tk[0], 1)
        tk[1] += 1

    def barrier(self):
        for e in ENGS:
            self.step(e, deps=[x for x in ENGS if x != e], body=lambda eng: None)

    def const(self, name, arr):
        if name in self.consts:
            return self.consts[name][0]
        dtype = BF16 if arr.dtype == ml_dtypes.bfloat16 else F32
        h = self.nc.alloc_sbuf_tensor("c_" + name, list(arr.shape), dtype)
        d = self.nc.inline_tensor(np.ascontiguousarray(arr), name="cd_" + name)
        self.consts[name] = (h, d)
        return h

    def c(self, name):
        return self.consts[name][0]

    def load_consts(self):
        cl = self.dsem("cload")
        items = list(self.consts.items())

        def body(sy):
            for name, (h, d) in items:
                sy.dma_start(out=h[:], in_=d.ap()).then_inc(cl[0], 16)
                cl[1] += 16
        self.step("sync", body=body)

    def debug_out(self, name, src_fn, deps=None):
        if name not in self.want_dbg:
            return
        src = src_fn() if callable(src_fn) else src_fn
        dd = self.nc.dram_tensor("dbg_" + name, list(src.shape),
                                 src.dtype, kind="ExternalOutput").ap()
        sem = self.dsem("dbgst")

        def body(sy):
            sy.dma_start(out=dd, in_=src).then_inc(sem[0], 16)
            sem[1] += 16
            sy.wait_ge(sem[0], sem[1])
        if deps is None:
            deps = [e for e in ENGS if e != "sync"]
        self.step("sync", deps=deps, body=body)
        self.dbgmap[name] = True


class Arena:
    def __init__(self, nc, words):
        self.h = nc.alloc_sbuf_tensor("ARENA", [128, words], F32)
        self.words = words
        self.stack = []
        self.off = 0
        self.hwm = 0

    def push(self, name):
        self.stack.append((name, self.off))

    def pop(self, name=None):
        nm, start = self.stack.pop()
        if name is not None:
            assert nm == name, (nm, name)
        self.off = start

    def _view(self, off, shape, dtype):
        n = int(np.prod(shape[1:]))
        wpe = 1 if dtype == F32 else 0.5
        nw = int(np.ceil(n * wpe))
        ap = self.h.ap()[:, off:off + nw]
        if dtype == BF16:
            ap = ap.bitcast(BF16)
        if len(shape) > 2:
            names = " ".join(f"d{i}" for i in range(1, len(shape)))
            ap = ap.rearrange(f"p ({names}) -> p {names}",
                              **{f"d{i}": shape[i] for i in range(1, len(shape))})
        if shape[0] != 128:
            ap = ap[0:shape[0]]
        return ap

    def f32(self, shape):
        nw = int(np.prod(shape[1:]))
        off = self.off
        self.off += nw
        self.hwm = max(self.hwm, self.off)
        assert self.off <= self.words, f"arena overflow: {self.off} > {self.words}"
        return self._view(off, shape, F32)

    def bf16(self, shape):
        n = int(np.prod(shape[1:]))
        assert n % 2 == 0
        off = self.off
        self.off += n // 2
        self.hwm = max(self.hwm, self.off)
        assert self.off <= self.words, f"arena overflow: {self.off} > {self.words}"
        return self._view(off, shape, BF16)


def build_program(nc, want_dbg=(), upto="all"):
    es = contextlib.ExitStack()

    def din(name, shape):
        return nc.dram_tensor(name, list(shape), F32, kind="ExternalInput").ap()

    w = {}
    for n, shp in [
        ("xT", (D, T)),
        ("norm_in_w", (1, D)), ("conv_w", (D, 4)), ("conv_b", (128, 8)),
        ("Wq", (D, 512)), ("Wk", (D, 512)), ("Wv", (D, 512)),
        ("Wo", (512, D)), ("Wspike", (D, 512)),
        ("Wau", (D, 64)), ("bau", (64, 1)), ("Wad", (64, 512)),
        ("bad_pg", (128, 4)), ("asp_pg", (128, 4)), ("bsp_pg", (128, 4)),
        ("Wbeta", (D, HN)), ("bbeta", (HN, 1)), ("hnw_pg", (128, 4)),
        ("Wu1", (D, 64)), ("bu1", (64, 1)), ("Wu2", (64, D)), ("bu2_pg", (128, 8)),
        ("ff_norm_w", (1, D)), ("Wff1", (D, DFF)), ("Wff3", (D, DFF)),
        ("Wff2", (DFF, D)),
        ("g1", (1, D)), ("g2", (1, D)), ("ebias", (HN, 1)),
        ("fWqT", (D, D)), ("fWkcT", (D, DC)), ("fWvcT", (D, DC)),
        ("fWoT", (D, D)), ("fWrT", (D, HN)),
        ("sW1T", (D, D)), ("sW2T", (D, D)), ("sW3T", (D, D)),
        ("eW1T", (HN, D, D)), ("eW2T", (HN, D, D)), ("eW3T", (HN, D, D)),
    ]:
        w[n] = din(n, shp)
    out_d = nc.dram_tensor("outT", [D, T], F32, kind="ExternalOutput").ap()

    with nc.Block() as block:
        k = K(nc, block, es)
        k.want_dbg = set(want_dbg)
        k.upto = upto
        _build(k, w, out_d)
    es.close()
    return nc


def _build(k, w, out_d):
    nc = k.nc

    k.const("ones_col", np.ones((128, 1), np.float32))
    k.const("ones_row", np.ones((1, 128), np.float32))
    k.const("ident_bf", np.eye(128, dtype=ml_dtypes.bfloat16))
    k.const("ident_f32", np.eye(128, dtype=np.float32))
    k.const("eps_ln", np.full((128, 1), 1e-5, np.float32))
    k.const("eps_rms", np.full((128, 1), 1e-6, np.float32))
    k.const("zero_col", np.zeros((128, 1), np.float32))
    k.const("p9col", np.full((128, 1), 0.9, np.float32))
    cm = np.ones((128, TS), np.float32)
    cm[:, ::CH] = 0.0
    k.const("chmask", cm)
    st64 = np.tril(np.ones((64, 64), np.float32), -1)
    k.const("stril2", np.concatenate([st64, st64], 0))
    k.const("striu2", np.concatenate([st64.T, st64.T], 0))
    iu = np.triu(np.ones((64, 64), np.float32))
    k.const("itriu2", np.concatenate([iu, iu], 0))
    i64 = np.eye(64, dtype=np.float32)
    k.const("ident2", np.concatenate([i64, i64], 0))
    for g in range(4):
        e = np.zeros((128, 8), np.float32)
        e[0:64, 2 * g] = 1.0
        e[64:128, 2 * g + 1] = 1.0
        k.const(f"esel{g}", e)
        k.const(f"eselb{g}", e.astype(ml_dtypes.bfloat16))
        eb = np.zeros((8, 128), np.float32)
        eb[2 * g, 0:64] = 1.0
        eb[2 * g + 1, 64:128] = 1.0
        k.const(f"ebc{g}", eb)
    for e in range(HN):
        eb = np.zeros((8, 128), np.float32)
        eb[e, :] = 1.0
        k.const(f"erow{e}", eb)
    tt = np.arange(TS) % SBLK
    k.const("pwrow", np.broadcast_to(
        (0.9 ** (-tt.astype(np.float64))).astype(np.float32)[None, :],
        (128, TS)).copy())

    x_s = nc.alloc_sbuf_tensor("x_s", [128, NPT, T], F32)
    sst = nc.alloc_sbuf_tensor("sst", [128, 4, 64], F32)
    stash = nc.alloc_sbuf_tensor("stash", [128, NPT, 3], F32)
    mcar = nc.alloc_sbuf_tensor("mcar", [128, 4], F32)
    dlast = nc.alloc_sbuf_tensor("dlast", [128, 4], F32)
    u0 = nc.alloc_sbuf_tensor("u0", [128, 4], F32)
    smallw = {}
    for n, shp, dt in [("Wau", (128, 8, 64), BF16), ("Wad", (64, 512), BF16),
                       ("Wbeta", (128, 8, 8), BF16), ("Wu1", (128, 8, 64), BF16),
                       ("Wu2", (64, 1024), BF16),
                       ("convw", (128, 8, 4), F32), ("convb", (128, 8), F32),
                       ("bau", (64, 1), F32), ("bu1", (64, 1), F32),
                       ("bbeta", (HN, 1), F32), ("ebias", (HN, 1), F32),
                       ("bad_pg", (128, 4), F32), ("asp_pg", (128, 4), F32),
                       ("bsp_pg", (128, 4), F32), ("hnw_pg", (128, 4), F32),
                       ("bu2_pg", (128, 8), F32)]:
        smallw[n] = nc.alloc_sbuf_tensor("sw_" + n, list(shp), dt)
    k.smallw = smallw
    k.x_s, k.w, k.out_d = x_s, w, out_d
    k.stash, k.sst, k.mcar, k.dlast = stash, sst, mcar, dlast
    k.u0 = u0

    rem = nc.sbuf_bytes_remaining
    arena_words = rem // 4 - 64
    k.A = Arena(nc, arena_words)
    k.arena_words = arena_words
    k.ROWS = k.A.f32([1, 8, 512])

    XL = k.dsem("xload")
    SW = k.dsem("swload")

    def p0_sync(sy):
        sy.dma_start(out=x_s[:], in_=w["xT"].rearrange("(i p) t -> p i t", p=128)).then_inc(XL[0], 16)
        XL[1] += 16
        sy.dma_start(out=smallw["convw"][:], in_=w["conv_w"].rearrange("(i p) c -> p i c", p=128)).then_inc(SW[0], 16)
        sy.dma_start(out=smallw["convb"][:], in_=w["conv_b"]).then_inc(SW[0], 16)
        for nm, src in [("bau", "bau"), ("bu1", "bu1"), ("bbeta", "bbeta"),
                        ("ebias", "ebias"), ("bad_pg", "bad_pg"),
                        ("asp_pg", "asp_pg"), ("bsp_pg", "bsp_pg"),
                        ("hnw_pg", "hnw_pg"), ("bu2_pg", "bu2_pg")]:
            sy.dma_start(out=smallw[nm][:], in_=w[src]).then_inc(SW[0], 16)
        SW[1] += 16 * 11
    k.step("sync", body=p0_sync)

    def p0_gp(g):
        for nm in ["Wau", "Wbeta", "Wu1"]:
            g.dma_start(out=smallw[nm][:], in_=w[nm].rearrange("(i p) f -> p i f", p=128)).then_inc(SW[0], 16)
        g.dma_start(out=smallw["Wad"][:], in_=w["Wad"]).then_inc(SW[0], 16)
        g.dma_start(out=smallw["Wu2"][:], in_=w["Wu2"]).then_inc(SW[0], 16)
        SW[1] += 16 * 5
    k.step("gpsimd", body=p0_gp)
    k.load_consts()
    k.step("vector", dma=[(XL, 16), (SW, SW[1]), (k.dsem("cload"), k.dsem("cload")[1])],
           body=lambda v: (v.memset(sst[:], 0.0), v.memset(mcar[:], 0.0),
                           v.memset(stash[:], 0.0)))
    k.barrier()

    nseg = 1 if k.upto == "seg0" else NSEG
    for s in range(nseg):
        _mixer_segment(k, s)
    k.debug_out("x_mid", lambda: x_s[:, 0, :])

    if k.upto not in ("seg0", "mixer"):
        _ffn_block(k)

    SO = k.dsem("soutd")

    def store(sy):
        sy.dma_start(out=out_d.rearrange("(i p) t -> p i t", p=128), in_=x_s[:]).then_inc(SO[0], 16)
        SO[1] += 16
        sy.wait_ge(SO[0], SO[1])
    k.step("sync", deps=[e for e in ENGS if e != "sync"], body=store)


def _ln_rows(k, rows, xt, seg_n, eps_name, rms_fold, sq1, psA):
    ocol = k.c("ones_col")

    def mm_s(te):
        for i in range(NPT):
            te.matmul(psA[0:1, 0, :], ocol[:], xt(i), start=(i == 0), stop=(i == NPT - 1))
    k.step("tensor", deps=ALL, body=mm_s)
    for i in range(NPT):
        k.step("scalar", deps=("tensor",) if i == 0 else ("tensor", "scalar"),
               body=lambda sc, i=i: sc.activation(sq1[:], xt(i), AF.Square))
        k.step("tensor", deps=("scalar",),
               body=lambda te, i=i: te.matmul(psA[0:1, 1, :], ocol[:], sq1[:],
                                              start=(i == 0), stop=(i == NPT - 1)))
    r = rows

    def v1(v):
        v.tensor_scalar(r[0:1, 4, :], psA[0:1, 0, :], 1.0 / D, None, op0=ALU.mult)
        v.tensor_scalar(r[0:1, 5, :], psA[0:1, 1, :], 1.0 / D, None, op0=ALU.mult)
        v.drain()
        v.tensor_tensor(r[0:1, 6, :], r[0:1, 4, :], r[0:1, 4, :], op=ALU.mult)
        v.drain()
        v.tensor_tensor(r[0:1, 5, :], r[0:1, 5, :], r[0:1, 6, :], op=ALU.subtract)
    k.step("vector", deps=("tensor",), body=v1)
    k.step("scalar", deps=("vector",), body=lambda sc: sc.activation(
        r[0:1, 6, :], r[0:1, 5, :], AF.Sqrt, bias=k.c(eps_name)[0:1]))

    def v2(v):
        v.reciprocal(r[0:1, 0, :], r[0:1, 6, :])
        v.drain()
        v.tensor_tensor(r[0:1, 7, :], r[0:1, 4, :], r[0:1, 0, :], op=ALU.mult)
        v.drain()
        v.tensor_scalar(r[0:1, 1, :], r[0:1, 7, :], -1.0, None, op0=ALU.mult)
        if rms_fold:
            v.tensor_tensor(r[0:1, 6, :], r[0:1, 0, :], r[0:1, 0, :], op=ALU.mult)
            v.drain()
            v.tensor_tensor(r[0:1, 6, :], r[0:1, 6, :], r[0:1, 5, :], op=ALU.mult)
    k.step("vector", deps=("scalar",), body=v2)
    if rms_fold:
        k.step("scalar", deps=("vector",), body=lambda sc: sc.activation(
            r[0:1, 7, :], r[0:1, 6, :], AF.Sqrt, bias=k.c("eps_rms")[0:1]))

        def v3(v):
            v.reciprocal(r[0:1, 7, :], r[0:1, 7, :])
            v.drain()
            v.tensor_tensor(r[0:1, 2, :], r[0:1, 0, :], r[0:1, 7, :], op=ALU.mult)
            v.tensor_tensor(r[0:1, 3, :], r[0:1, 1, :], r[0:1, 7, :], op=ALU.mult)
        k.step("vector", deps=("scalar",), body=v3)


def _mixer_segment(k, s):
    nc, A, w = k.nc, k.A, k.w
    x_s = k.x_s
    seg = ts_(s, TS)
    A.push(f"segP{s}")
    rows = k.ROWS
    srow = A.f32([HN, 6, TS])
    k.o_gseg = A.off
    gseg = A.bf16([128, NPT, TS])
    k.o_oseg = A.off
    oseg = A.f32([128, 4, TS])
    sq1 = A.f32([128, TS])

    with nc.psum_tensor(f"psA{s}", [1, 2, TS], F32) as psA:
        _ln_rows(k, rows, lambda i: x_s[:, i, seg], TS, "eps_ln", True, sq1, psA)

    A.push("early")
    hq = A.f32([128, 4, TS])
    hk = A.f32([128, 4, TS])
    hv = A.f32([128, 4, TS])
    lam = A.f32([128, 4, TS])
    tau = A.f32([128, TS, 4])
    spk = A.bf16([128, 4, TS])

    A.push("convproj")
    k.o_hseg = A.off
    hseg = A.bf16([128, NPT, TS])
    k.o_lnseg = A.off
    lnseg = A.bf16([128, NPT, TS])
    k.o_wbuf = A.off
    wbuf = A.bf16([128, NPT, 512])
    k.o_pre = A.off
    pre = A.f32([128, 4 + TS])
    aub = A.bf16([64, TS])
    nwrow = rows[0:1, 6, :].rearrange("a b -> a b") if False else None
    nwr = A.f32([1, D])
    NW = k.dsem(f"nw{s}")

    def ldnw(sy):
        sy.dma_start(out=nwr, in_=w["norm_in_w"]).then_inc(NW[0], 16)
        NW[1] += 16
    k.step("sync", body=ldnw)

    _conv_ln(k, s, seg, rows, hseg, lnseg, pre, sq1, nwr, NW, gseg, aub)
    if s == 0:
        k.debug_out("hseg0", lambda: hseg[:, 0, :])
        k.debug_out("lnseg0", lambda: lnseg[:, 0, :])
    dlt = A._view(k.o_oseg, [128, TS, 4], F32)
    utr = A._view(k.o_lnseg, [128, TS, 4], F32)
    _projections(k, s, seg, hseg, lnseg, hq, hk, hv, lam, tau, srow, gseg,
                 sq1, wbuf, aub, dlt, utr, spk)
    if s == 0:
        k.debug_out("gate0", lambda: gseg[:, 0, :])
    A.pop("convproj")

    A.push("chunk")
    _chunks(k, s, hq, hk, hv, lam, tau, srow, oseg, spk, utr, sq1)
    A.pop("chunk")
    A.pop("early")
    if s == 0:
        k.debug_out("spk0", lambda: spk[:, 0, :])
        k.debug_out("beta0", lambda: srow[0:HN, 1, :])
    if s == 0:
        k.debug_out("oseg0", lambda: oseg[:, 0, :])

    A.push("tail")
    _tail(k, s, seg, rows, srow, gseg, oseg, sq1)
    A.pop("tail")
    if s == 0:
        pass
    A.pop(f"segP{s}")
    k.barrier()
    if s == 0:
        k.debug_out("xseg0", lambda: x_s[:, 0, seg])


def _conv_ln(k, s, seg, rows, hseg, lnseg, pre, acc, nwr, NW, gseg, aub):
    nc, sw, x_s = k.nc, k.smallw, k.x_s
    with nc.psum_tensor(f"pcv{s}", [128, 4, TS], F32) as pb:
        for i in range(NPT):
            def bc(te, i=i):
                te.matmul(pb[:, 0, :], nwr[0:1, ts_(i, 128)], rows[0:1, 2, :],
                          start=True, stop=True)
                te.matmul(pb[:, 1, :], nwr[0:1, ts_(i, 128)], rows[0:1, 3, :],
                          start=True, stop=True)
                te.matmul(pb[:, 2, :], k.c("ones_row")[:], rows[0:1, 0, :],
                          start=True, stop=True)
                te.matmul(pb[:, 3, :], k.c("ones_row")[:], rows[0:1, 1, :],
                          start=True, stop=True)
            k.step("tensor", deps=("vector", "scalar"),
                   dma=[(NW, NW[1])] if i == 0 else (), body=bc)

            def vb(v, i=i):
                if s == 0:
                    v.memset(pre[:, 0:4], 0.0)
                else:
                    v.tensor_copy(pre[:, 1:4], k.stash[:, i, :])
                v.tensor_tensor(pre[:, 4:], x_s[:, i, seg], pb[:, 0, :], op=ALU.mult)
                v.drain()
                v.tensor_tensor(pre[:, 4:], pre[:, 4:], pb[:, 1, :], op=ALU.add)
                v.tensor_tensor(acc[:], x_s[:, i, seg], pb[:, 2, :], op=ALU.mult)
                v.drain()
                v.tensor_tensor(lnseg[:, i, :], acc[:], pb[:, 3, :], op=ALU.add)
                v.tensor_copy(k.stash[:, i, :], pre[:, 1 + TS:4 + TS])
                v.drain()
                v.tensor_scalar(acc[:], pre[:, 1:1 + TS], sw["convw"][:, i, 0:1],
                                None, op0=ALU.mult)
                v.drain()
                for j in range(1, 4):
                    v.scalar_tensor_tensor(acc[:], pre[:, 1 + j:1 + j + TS],
                                           sw["convw"][:, i, j:j + 1], acc[:],
                                           op0=ALU.mult, op1=ALU.add)
                    v.drain()
            k.step("vector", deps=("tensor",), body=vb)
            k.step("scalar", deps=("vector",), body=lambda sc, i=i: sc.activation(
                hseg[:, i, :], acc[:], AF.Silu, bias=sw["convb"][:, i:i + 1]))

    with nc.psum_tensor(f"pg1{s}", [64, TS], F32) as pg1:
        def mm4(te):
            for i in range(NPT):
                te.matmul(pg1[:, :], sw["Wu1"][:, i, :], lnseg[:, i, :],
                          start=(i == 0), stop=(i == NPT - 1))
        k.step("tensor", deps=("vector", "scalar"), body=mm4)
        k.step("scalar", deps=("tensor",), body=lambda sc: sc.activation(
            aub[:], pg1[:, :], AF.Silu, bias=sw["bu1"][:]))
    with nc.psum_tensor(f"pg2{s}", [128, 4, TS], F32) as pg2:
        for half in range(2):
            def mm5(te, half=half):
                for i in range(4):
                    te.matmul(pg2[:, i, :], sw["Wu2"][:, ts_(half * 4 + i, 128)],
                              aub[:], start=True, stop=True)
            k.step("tensor", deps=("scalar",), body=mm5)

            def ev5(sc, half=half):
                for i in range(4):
                    ii = half * 4 + i
                    sc.activation(gseg[:, ii, :], pg2[:, i, :], AF.Sigmoid,
                                  bias=sw["bu2_pg"][:, ii:ii + 1])
            k.step("scalar", deps=("tensor",), body=ev5)


def _projections(k, s, seg, hseg, lnseg, hq, hk, hv, lam, tau, srow, gseg,
                 sq1, wbuf, aub, dlt, utr, spk):
    nc, A, w, sw = k.nc, k.A, k.w, k.smallw
    WL = k.dsem(f"wl{s}")
    k.t_pb = {}

    with nc.psum_tensor(f"pqk{s}", [128, 8, TS], F32) as pp:
        specs = [("Wspike", tau, "drive"), ("Wq", hq, "norm"),
                 ("Wk", hk, "norm"), ("Wv", hv, "plain")]
        t_evp = {}
        t_mm = {}
        for wi, (wname, dst, kind) in enumerate(specs):
            def ld(g, wname=wname):
                g.dma_start(out=wbuf, in_=w[wname].rearrange("(i p) f -> p i f", p=128)
                            ).then_inc(WL[0], 16)
                WL[1] += 16
            k.step("gpsimd", deps=("tensor", "scalar") if wi == 0 else (),
                   deps_at=[("tensor", t_mm.get(wi - 1, 0))], body=ld)
            need = WL[1]
            hp = wi % 2

            def mm(te, hp=hp):
                for m in range(4):
                    for i in range(NPT):
                        te.matmul(pp[:, 4 * hp + m, :], wbuf[:, i, ts_(m, 128)],
                                  hseg[:, i, :],
                                  start=(i == 0), stop=(i == NPT - 1))
            k.step("tensor", deps=("scalar", "vector") if wi == 0 else (),
                   deps_at=[("scalar", t_evp.get(wi - 2, 0))],
                   dma=[(WL, need)], body=mm)
            t_mm[wi] = k.tick["tensor"][1]

            if kind == "drive":
                def ev(sc, hp=hp):
                    for m in range(4):
                        sc.activation(tau[:, :, m], pp[:, 4 * hp + m, :], AF.Identity,
                                      bias=k.c("zero_col")[:])
                k.step("scalar", deps_at=[("tensor", t_mm[wi])], body=ev)
            else:
                def ev(sc, dst=dst, hp=hp):
                    for m in range(4):
                        sc.activation(dst[:, m, :], pp[:, 4 * hp + m, :], AF.Identity,
                                      bias=k.c("zero_col")[:])
                k.step("scalar", deps_at=[("tensor", t_mm[wi])], body=ev)
            t_evp[wi] = k.tick["scalar"][1]
            if kind == "drive":
                _dscan(k, s, tau, dlt)
            if wi >= 1:
                _pscan_block(k, s, wi - 1, tau, dlt, utr)
    _l2norm(k, s, 0, hq, srow, sq1)
    _l2norm(k, s, 1, hk, srow, sq1)

    SWs = k.dsem("swload")
    with nc.psum_tensor(f"pau{s}", [64, TS], F32) as pau:
        def mm1(te):
            for i in range(NPT):
                te.matmul(pau[:, :], sw["Wau"][:, i, :], hseg[:, i, :],
                          start=(i == 0), stop=(i == NPT - 1))
        k.step("tensor", deps=("scalar", "vector"), dma=[(SWs, SWs[1])], body=mm1)
        k.step("scalar", deps=("tensor",), body=lambda sc: sc.activation(
            aub[:], pau[:, :], AF.Silu, bias=sw["bau"][:]))
    with nc.psum_tensor(f"pad{s}", [128, 4, TS], F32) as pad:
        def mm2(te):
            for m in range(4):
                te.matmul(pad[:, m, :], sw["Wad"][:, ts_(m, 128)], aub[:],
                          start=True, stop=True)
        k.step("tensor", deps=("scalar",), body=mm2)

        def ev2(sc):
            for m in range(4):
                sc.activation(lam[:, m, :], pad[:, m, :], AF.Identity,
                              bias=sw["bad_pg"][:, m:m + 1])
        k.step("scalar", deps=("tensor",), body=ev2)

    with nc.psum_tensor(f"pbt{s}", [HN, TS], F32) as pbt:
        def mm3(te):
            for i in range(NPT):
                te.matmul(pbt[:, :], sw["Wbeta"][:, i, :], hseg[:, i, :],
                          start=(i == 0), stop=(i == NPT - 1))
        k.step("tensor", deps=("scalar",), body=mm3)
        k.step("scalar", deps=("tensor",), body=lambda sc: sc.activation(
            srow[0:HN, 0, :], pbt[:, :], AF.Identity, bias=sw["bbeta"][:]))

    if s == 0:
        k.debug_out("hq0", lambda: hq[:, 0, :])
        k.debug_out("hv0", lambda: hv[:, 0, :])
        k.debug_out("ab0", lambda: lam[:, 0, :])

    for b in range(3, 8):
        _pscan_block(k, s, b, tau, dlt, utr)


def _l2norm(k, s, tag, dst, srow, sq1):
    nc = k.nc
    with nc.psum_tensor(f"pl2{s}_{tag}", [HN, TS], F32) as pn:
        for g in range(4):
            k.step("scalar", deps=("scalar",) if g else ("scalar", "vector"),
                   body=lambda sc, g=g: sc.activation(sq1[:], dst[:, g, :], AF.Square))
            k.step("tensor", deps=("scalar",),
                   body=lambda te, g=g: te.matmul(pn[:, :], k.c(f"esel{g}")[:], sq1[:],
                                                  start=(g == 0), stop=(g == 3)))
        k.step("scalar", deps=("tensor",), body=lambda sc: sc.activation(
            srow[0:HN, 5, :], pn[:, :], AF.Sqrt, bias=k.c("zero_col")[0:HN]))

    def vr(v):
        v.tensor_scalar(srow[0:HN, 5, :], srow[0:HN, 5, :], 1e-6, None, op0=ALU.add)
        v.drain()
        v.reciprocal(srow[0:HN, 5, :], srow[0:HN, 5, :])
    k.step("vector", deps=("scalar",), body=vr)
    with nc.psum_tensor(f"pl2b{s}_{tag}", [128, 4, TS], F32) as pnb:
        def mmb(te):
            for g in range(4):
                te.matmul(pnb[:, g, :], k.c(f"ebc{g}")[:], srow[0:HN, 5, :],
                          start=True, stop=True)
        k.step("tensor", deps=("vector",), body=mmb)

        def vm(v):
            for g in range(4):
                v.tensor_tensor(dst[:, g, :], dst[:, g, :], pnb[:, g, :], op=ALU.mult)
        k.step("vector", deps=("tensor",), body=vm)


def _dscan(k, s, tau, dlt):
    def dscan(v):
        p9 = k.c("p9col")[:].to_broadcast([128, TS])
        for g in range(4):
            v.tensor_tensor_scan(dlt[:, :, g], p9, tau[:, :, g],
                                 k.mcar[:, g:g + 1], op0=ALU.mult, op1=ALU.add)
        v.drain()
        v.tensor_copy(k.dlast[:], dlt[:, TS - 1, :])
        for g in range(4):
            v.tensor_scalar(tau[:, :, g], dlt[:, :, g], -1.0, 0.5,
                            op0=ALU.mult, op1=ALU.add)
        v.drain()
        for g in range(4):
            v.tensor_tensor(tau[:, :, g], tau[:, :, g], k.c("pwrow")[:], op=ALU.mult)
        v.drain()
        v.tensor_tensor(dlt[:, 0:TS - 1, :], tau[:, 0:TS - 1, :], tau[:, 1:TS, :],
                        op=ALU.subtract)
        v.drain()
        v.tensor_scalar(k.u0[:], tau[:, 0, :], -1.0, None, op0=ALU.mult)
    k.step("vector", deps=("scalar",), body=dscan)
    k.t_dscan = k.tick["vector"][1]


def _pscan_block(k, s, bi, tau, dlt, utr):
    t0, t1 = bi * 64, bi * 64 + 64

    def pq(g):
        ysc = dlt[:, TS - 1, :]
        for t in range(t0, t1):
            blk = t % SBLK
            sig = 0.45 * (0.9 ** (-(blk + 1.0)))
            uprev = k.u0[:] if t == 0 else utr[:, t - 1, :]
            g.tensor_scalar(ysc, uprev, 0.0, -sig, op0=ALU.is_gt, op1=ALU.mult)
            g.tensor_tensor(utr[:, t, :], uprev, ysc, op=ALU.add)
            if t < TS - 1:
                if blk == SBLK - 1:
                    g.tensor_tensor(utr[:, t, :], utr[:, t, :], tau[:, t, :],
                                    op=ALU.add)
                    g.tensor_scalar(utr[:, t, :], utr[:, t, :], 0.9 ** SBLK,
                                    None, op0=ALU.mult)
                    g.tensor_tensor(utr[:, t, :], utr[:, t, :], tau[:, t + 1, :],
                                    op=ALU.subtract)
                else:
                    g.tensor_tensor(utr[:, t, :], utr[:, t, :], dlt[:, t, :],
                                    op=ALU.add)
        if t1 == TS:
            blk = (TS - 1) % SBLK
            g.tensor_tensor(k.mcar[:], utr[:, TS - 1, :], tau[:, TS - 1, :],
                            op=ALU.add)
            g.tensor_scalar(k.mcar[:], k.mcar[:], 0.9 ** blk, None, op0=ALU.mult)
            g.tensor_tensor(k.mcar[:], k.mcar[:], k.dlast[:], op=ALU.add)
    k.step("gpsimd", deps=(),
           deps_at=[("vector", k.t_dscan)] if bi == 0 else [], body=pq)
    k.t_pb[bi] = k.tick["gpsimd"][1]


def _sp_half(k, half, utr, spk):
    if half == 0:
        def sp(v):
            for g in range(4):
                v.tensor_scalar(spk[:, g, 0:1], k.u0[:, g:g + 1], 0.0, None,
                                op0=ALU.is_gt)
                v.tensor_scalar(spk[:, g, 1:256], utr[:, 0:255, g], 0.0, None,
                                op0=ALU.is_gt)
        k.step("vector", deps=(), deps_at=[("gpsimd", k.t_pb[3])], body=sp)
    else:
        def sp(v):
            for g in range(4):
                v.tensor_scalar(spk[:, g, 256:512], utr[:, 255:511, g], 0.0, None,
                                op0=ALU.is_gt)
        k.step("vector", deps=("gpsimd",), body=sp)


def _alpha_beta(k, s, lam, spk, srow, sq1, half):
    nc, sw = k.nc, k.smallw
    H = 256
    hh = slice(half * H, half * H + H)
    with nc.psum_tensor(f"pab{s}_{half}", [HN, 2, H], F32) as pab:
        def mma(te):
            for g in range(4):
                te.matmul(pab[:, 0, :], k.c(f"eselb{g}")[:], spk[:, g, hh],
                          start=(g == 0), stop=(g == 3))
        k.step("tensor", deps=("vector", "scalar"), body=mma)
        for g in range(4):
            k.step("vector", deps=("vector",) if g else ("vector", "tensor"),
                   body=lambda v, g=g: v.tensor_scalar(
                       sq1[:, 0:H], spk[:, g, hh], sw["bsp_pg"][:, g:g + 1],
                       None, op0=ALU.mult))
            k.step("tensor", deps=("vector",),
                   body=lambda te, g=g: te.matmul(pab[:, 1, :], k.c(f"esel{g}")[:],
                                                  sq1[:, 0:H], start=(g == 0),
                                                  stop=(g == 3)))

        def vact(v):
            v.tensor_scalar(srow[0:HN, 2, hh], pab[0:HN, 0, :], 0.0, None,
                            op0=ALU.is_gt)
            v.tensor_tensor(srow[0:HN, 3, hh], srow[0:HN, 0, hh], pab[0:HN, 1, :],
                            op=ALU.add)
        k.step("vector", deps=("tensor",), body=vact)
    k.step("scalar", deps=("vector",), body=lambda sc: sc.activation(
        srow[0:HN, 1, hh], srow[0:HN, 3, hh], AF.Sigmoid,
        bias=k.c("zero_col")[0:HN]))
    k.step("vector", deps=("scalar",), body=lambda v: v.tensor_tensor(
        srow[0:HN, 1, hh], srow[0:HN, 1, hh], srow[0:HN, 2, hh], op=ALU.mult))

    def vz(v):
        for g in range(4):
            v.scalar_tensor_tensor(lam[:, g, hh], spk[:, g, hh],
                                   sw["asp_pg"][:, g:g + 1],
                                   lam[:, g, hh], op0=ALU.mult, op1=ALU.add)
        v.drain()
        for g in range(4):
            v.tensor_scalar(lam[:, g, hh], lam[:, g, hh], -60.0, None, op0=ALU.max)
    k.step("vector", deps=("scalar",), body=vz)

    def sact(sc):
        for g in range(4):
            sc.activation(lam[:, g, hh], lam[:, g, hh], AF.Sigmoid)
    k.step("scalar", deps=("vector",), body=sact)

    def sln(sc):
        for g in range(4):
            sc.activation(lam[:, g, hh], lam[:, g, hh], AF.Ln)
    k.step("scalar", deps=("scalar",), body=sln)

    with nc.psum_tensor(f"pabc{s}_{half}", [128, 4, H], F32) as pbc:
        def mmb(te):
            for g in range(4):
                te.matmul(pbc[:, g, :], k.c(f"ebc{g}")[:], srow[0:HN, 2, hh],
                          start=True, stop=True)
        k.step("tensor", deps=("vector", "scalar"), body=mmb)

        def vmask(v):
            for g in range(4):
                v.tensor_tensor(lam[:, g, hh], lam[:, g, hh], pbc[:, g, :],
                                op=ALU.mult)
        k.step("vector", deps=("tensor", "scalar"), body=vmask)


def _chunks(k, s, hq, hk, hv, lam, tau, srow, oseg, spk, utr, sq1):
    nc, A = k.nc, k.A
    SC = A._view(k.o_hseg, [128, 4, TS], F32)
    Xst = A._view(k.o_wbuf, [128, NU, CH], F32)
    P = k.o_pre
    Aex = A._view(P, [128, 256], F32)
    Rex = A._view(P + 256, [128, 256], F32)
    shalf = A._view(P + 512, [128, 4, NCHS], F32)
    afr = A._view(P + 544, [128, 4, NCHS], F32)
    rcr = A._view(P + 576, [128, 4, NCHS], F32)
    Pst = A._view(P + 608, [128, 8, 64], F32)
    Qst = A._view(P + 1120, [128, 8, 64], F32)
    s0t = A._view(k.o_lnseg, [128, 4, 64], F32)
    rvt = A._view(k.o_lnseg + 256, [128, 4, 64], F32)
    ust = A._view(k.o_lnseg + 512, [128, 4, 64], F32)
    tmp4 = A._view(k.o_lnseg + 768, [128, 4, 64], F32)
    phiT = tau.rearrange("p t g -> p (t g)").rearrange(
        "p (c g l) -> p c g l", c=NCHS, g=4, l=CH)
    kseg, qseg, wseg = hk, hq, lam
    TP = [(0, 0), (64, 64)]

    es = contextlib.ExitStack()
    pm = es.enter_context(nc.psum_tensor(f"pcm{s}", [128, 32, 64], F32))
    ptr = pm.rearrange("p a l -> p (a l)").rearrange("p (a l) -> p a l", a=16, l=128)

    for half in range(2):
        _sp_half(k, half, utr, spk)
        _alpha_beta(k, s, lam, spk, srow, sq1, half)
        _chunks_pre(k, s, half, lam, hk, hq, srow, SC, Aex, Rex,
                    shalf, afr, rcr, kseg, qseg, wseg)
        if half == 0 and s == 0:
            nogp = ("vector", "scalar", "tensor")
            k.debug_out("kt0", lambda: kseg[:, 0, 0:64], deps=nogp)
            k.debug_out("qt0", lambda: qseg[:, 0, 0:64], deps=nogp)
            k.debug_out("wt0", lambda: wseg[:, 0, 0:64], deps=nogp)
        for bi in ((0, 1) if half == 0 else (2, 3)):
            _chunks_par(k, s, bi, pm, ptr, kseg, qseg, wseg, hv,
                        Pst, Qst, Xst, phiT)

    if s == 0:
        k.debug_out("sc0", lambda: SC[:, 0, :])
        k.debug_out("af0", lambda: afr[:, 0, :])
        k.debug_out("rc0", lambda: rcr[:, 0, :])
        k.debug_out("xst0", lambda: Xst[:, 0, :])
        k.debug_out("phi0", lambda: phiT[:, 0, 0, :])
        k.debug_out("vtm0", lambda: hv[:, 0, 0:64])
        k.debug_out("wtm0", lambda: wseg[:, 0, 0:64])

    vtm, wtm = hv, wseg
    for c in range(NCHS):
        cs = ts_(c, CH)

        def v_s0(v, c=c):
            for g in range(4):
                v.tensor_scalar(s0t[:, g, :], k.sst[:, g, :], rcr[:, g, c:c + 1],
                                None, op0=ALU.mult)
        k.step("vector", deps=("tensor", "scalar", "vector"), body=v_s0)

        def mm_ks(te, c=c):
            for g in range(4):
                for h in range(2):
                    hs = ts_(h, 64)
                    te.matmul(pm[:, g, :][hs], kseg[:, g, cs][hs],
                              s0t[:, g, :][hs], start=True, stop=True,
                              tile_position=TP[h])
        k.step("tensor", deps=("vector",), body=mm_ks)

        def v_rv(v, c=c):
            for g in range(4):
                v.tensor_tensor(rvt[:, g, :], vtm[:, g, cs], pm[:, g, :],
                                op=ALU.subtract)
        k.step("vector", deps=("tensor",), body=v_rv)

        def mm_u(te, c=c):
            for g in range(4):
                uu = c * 4 + g
                for h in range(2):
                    hs = ts_(h, 64)
                    te.matmul(pm[:, 4 + g, :][hs], Xst[:, uu, :][hs],
                              rvt[:, g, :][hs], start=True, stop=True,
                              tile_position=TP[h])
        k.step("tensor", deps=("vector",), body=mm_u)

        def ev_u(sc, c=c):
            for g in range(4):
                sc.activation(ust[:, g, :], pm[:, 4 + g, :], AF.Identity,
                              bias=k.c("zero_col")[:])
        k.step("scalar", deps=("tensor",), body=ev_u)

        def mm_o(te, c=c):
            for g in range(4):
                for h in range(2):
                    hs = ts_(h, 64)
                    te.matmul(pm[:, 8 + g, :][hs], s0t[:, g, :][hs],
                              qseg[:, g, cs][hs], start=True, stop=False,
                              tile_position=TP[h])
                    te.matmul(pm[:, 8 + g, :][hs], ust[:, g, :][hs],
                              phiT[:, c, g, :][hs], start=False, stop=True,
                              tile_position=TP[h])
            for g in range(4):
                for h in range(2):
                    hs = ts_(h, 64)
                    te.matmul(pm[:, 12 + g, :][hs], wtm[:, g, cs][hs],
                              ust[:, g, :][hs], start=True, stop=True,
                              tile_position=TP[h])
        k.step("tensor", deps=("scalar",), body=mm_o)

        def ev_o(sc, c=c):
            for g in range(4):
                sc.activation(oseg[:, g, cs], pm[:, 8 + g, :], AF.Identity,
                              bias=k.c("zero_col")[:])
        k.step("scalar", deps=("tensor",), body=ev_o)

        def v_st(v, c=c):
            for g in range(4):
                v.tensor_scalar(tmp4[:, g, :], pm[:, 12 + g, :], rcr[:, g, c:c + 1],
                                None, op0=ALU.mult)
            v.drain()
            for g in range(4):
                v.scalar_tensor_tensor(k.sst[:, g, :], k.sst[:, g, :],
                                       afr[:, g, c:c + 1], tmp4[:, g, :],
                                       op0=ALU.mult, op1=ALU.add)
        k.step("vector", deps=("tensor",), body=v_st)
        if s == 0 and c == 0:
            k.debug_out("ust0", lambda: ust[:, 0, :])
            k.debug_out("sst0", lambda: k.sst[:, 0, :])
    es.close()


def _chunks_pre(k, s, half, lam, hk, hq, srow, SC, Aex, Rex,
                shalf, afr, rcr, kseg, qseg, wseg):
    nc = k.nc
    H = 256
    hh = slice(half * H, half * H + H)
    cr = slice(half * 4, half * 4 + 4)

    def vcs(v):
        msk = k.c("chmask")[:, hh]
        for g in range(4):
            v.tensor_tensor_scan(SC[:, g, hh], msk, lam[:, g, hh], 0.0,
                                 op0=ALU.mult, op1=ALU.add)
        v.drain()
        for g in range(4):
            v.tensor_scalar(shalf[:, g, cr],
                            SC[:, g, hh].rearrange("p (c l) -> p c l", l=CH)[:, :, CH - 1],
                            0.5, None, op0=ALU.mult)
        v.drain()
        for g in range(4):
            v.tensor_tensor(SC[:, g, hh].rearrange("p (c l) -> p c l", l=CH),
                            SC[:, g, hh].rearrange("p (c l) -> p c l", l=CH),
                            shalf[:, g, cr].unsqueeze(2).to_broadcast([128, 4, CH]),
                            op=ALU.subtract)
        v.drain()
        for g in range(4):
            v.tensor_scalar(SC[:, g, hh], SC[:, g, hh], 80.0, -80.0,
                            op0=ALU.min, op1=ALU.max)
    k.step("vector", deps=("vector", "tensor", "scalar"), body=vcs)

    def sexp0(sc):
        for g in range(4):
            sc.activation(afr[:, g, cr], shalf[:, g, cr], AF.Exp, scale=2.0)
            sc.activation(rcr[:, g, cr], shalf[:, g, cr], AF.Exp)
    k.step("scalar", deps=("vector",), body=sexp0)

    with nc.psum_tensor(f"pcb{s}_{half}", [128, 4, H], F32) as pbb:
        def mmbb(te):
            for g in range(4):
                te.matmul(pbb[:, g, :], k.c(f"ebc{g}")[:], srow[0:HN, 1, hh],
                          start=True, stop=True)
        k.step("tensor", deps=("vector", "scalar"), body=mmbb)
        for g in range(4):
            def sexp(sc, g=g):
                sc.activation(Aex[:], SC[:, g, hh], AF.Exp)
                sc.activation(Rex[:], SC[:, g, hh], AF.Exp, scale=-1.0)
            k.step("scalar", deps=("vector",) if g else ("vector", "tensor"),
                   body=sexp)

            def vkt(v, g=g):
                v.tensor_tensor(wseg[:, g, hh], hk[:, g, hh], Rex[:], op=ALU.mult)
                v.drain()
                v.tensor_tensor(wseg[:, g, hh], wseg[:, g, hh], pbb[:, g, :],
                                op=ALU.mult)
                v.tensor_tensor(kseg[:, g, hh], hk[:, g, hh], Aex[:], op=ALU.mult)
                v.tensor_tensor(qseg[:, g, hh], hq[:, g, hh], Aex[:], op=ALU.mult)
            k.step("vector", deps=("scalar",), body=vkt)


def _chunks_par(k, s, bi, pm, ptr, kseg, qseg, wseg, hv, Pst, Qst, Xst, phiT):
    units = [(bi * 8 + u) for u in range(8)]
    HVS = {"a": range(0, 4), "b": range(4, 8)}
    ms = lambda u: u if u < 4 else 4 + u
    qs = lambda u: 4 + u if u < 4 else 8 + u
    xs = lambda u: 16 + u if u < 4 else 20 + u

    def mmM(te, us):
        for u in us:
            uu = units[u]
            c, g = divmod(uu, 4)
            cs = ts_(c, CH)
            for h in range(2):
                hs = ts_(h, 64)
                te.matmul(pm[:, ms(u), :][hs], kseg[:, g, cs][hs],
                          wseg[:, g, cs][hs], start=True, stop=True,
                          tile_position=TP2[h])
                te.matmul(pm[:, qs(u), :][hs], wseg[:, g, cs][hs],
                          kseg[:, g, cs][hs], start=True, stop=True,
                          tile_position=TP2[h])
                te.matmul(pm[:, xs(u), :][hs], wseg[:, g, cs][hs],
                          qseg[:, g, cs][hs], start=True, stop=True,
                          tile_position=TP2[h])
    k.step("tensor", deps=("vector", "scalar"),
           body=lambda te: mmM(te, HVS["a"]))
    t_mmM_a = k.tick["tensor"][1]
    k.step("tensor", deps=(), body=lambda te: mmM(te, HVS["b"]))
    t_mmM_b = k.tick["tensor"][1]

    def evM(v, us):
        for u in us:
            uu = units[u]
            c, g = divmod(uu, 4)
            v.tensor_tensor(Qst[:, u, :], pm[:, ms(u), :], k.c("stril2")[:], op=ALU.mult)
            v.tensor_tensor(Pst[:, u, :], pm[:, qs(u), :], k.c("striu2")[:], op=ALU.mult)
            v.tensor_tensor(phiT[:, c, g, :], pm[:, xs(u), :], k.c("itriu2")[:],
                            op=ALU.mult)
        v.drain()
        for u in us:
            v.tensor_tensor(Xst[:, units[u], :], k.c("ident2")[:], Pst[:, u, :],
                            op=ALU.subtract)
    k.step("vector", deps_at=[("tensor", t_mmM_a)],
           body=lambda v: evM(v, HVS["a"]))
    t_ev_a = k.tick["vector"][1]
    k.step("vector", deps_at=[("tensor", t_mmM_b)],
           body=lambda v: evM(v, HVS["b"]))
    t_ev_b = k.tick["vector"][1]
    tA = ("vector", t_ev_a)
    tB = ("vector", t_ev_b)

    for lvl in range(1, 6):
        def mml(te, us):
            for u in us:
                for h in range(2):
                    hs = ts_(h, 64)
                    te.matmul(pm[:, ms(u), :][hs], Qst[:, u, :][hs],
                              Pst[:, u, :][hs], start=True, stop=True,
                              tile_position=TP2[h])
                    te.matmul(pm[:, qs(u), :][hs], Pst[:, u, :][hs],
                              Qst[:, u, :][hs], start=True, stop=True,
                              tile_position=TP2[h])
        k.step("tensor", deps_at=[tA], body=lambda te: mml(te, HVS["a"]))
        t_mml_a = k.tick["tensor"][1]
        k.step("tensor", deps_at=[tB], body=lambda te: mml(te, HVS["b"]))
        t_mml_b = k.tick["tensor"][1]

        def evl_s(sc):
            for u in HVS["a"]:
                sc.activation(Pst[:, u, :], pm[:, ms(u), :], AF.Identity,
                              bias=k.c("zero_col")[:])
                sc.activation(Qst[:, u, :], pm[:, qs(u), :], AF.Identity,
                              bias=k.c("zero_col")[:])
        k.step("scalar", deps_at=[("tensor", t_mml_a)], body=evl_s)
        t_evl_a = k.tick["scalar"][1]

        def evl_v(v):
            for u in HVS["b"]:
                v.tensor_copy(Pst[:, u, :], pm[:, ms(u), :])
                v.tensor_copy(Qst[:, u, :], pm[:, qs(u), :])
        k.step("vector", deps_at=[("tensor", t_mml_b)], body=evl_v)
        t_evl_b = k.tick["vector"][1]

        def mmx(te, us):
            for u in us:
                uu = units[u]
                for h in range(2):
                    hs = ts_(h, 64)
                    te.matmul(pm[:, xs(u), :][hs], k.c("ident2")[:][hs],
                              Xst[:, uu, :][hs], start=True, stop=False,
                              tile_position=TP2[h])
                    te.matmul(pm[:, xs(u), :][hs], Qst[:, u, :][hs],
                              Xst[:, uu, :][hs], start=False, stop=True,
                              tile_position=TP2[h])
        k.step("tensor", deps_at=[("scalar", t_evl_a)],
               body=lambda te: mmx(te, HVS["a"]))
        t_mmx_a = k.tick["tensor"][1]
        k.step("tensor", deps_at=[("vector", t_evl_b)],
               body=lambda te: mmx(te, HVS["b"]))
        t_mmx_b = k.tick["tensor"][1]

        def evx_s(sc):
            for u in HVS["a"]:
                sc.activation(Xst[:, units[u], :], pm[:, xs(u), :], AF.Identity,
                              bias=k.c("zero_col")[:])
        k.step("scalar", deps_at=[("tensor", t_mmx_a)], body=evx_s)
        tA = ("scalar", k.tick["scalar"][1])

        def evx_v(v):
            for u in HVS["b"]:
                v.tensor_copy(Xst[:, units[u], :], pm[:, xs(u), :])
        k.step("vector", deps_at=[("tensor", t_mmx_b)], body=evx_v)
        tB = ("vector", k.tick["vector"][1])

    def mmt2(te):
        for u, uu in enumerate(units):
            c, g = divmod(uu, 4)
            cs = ts_(c, CH)
            te.transpose(ptr[0:64, u, :], hv[:, g, cs], k.c("ident_f32")[:])
    k.step("tensor", deps_at=[tA, tB, ("vector", k.tick["vector"][1])], body=mmt2)

    def evt(v):
        for u, uu in enumerate(units):
            c, g = divmod(uu, 4)
            cs = ts_(c, CH)
            v.tensor_copy(hv[:, g, cs][0:64], ptr[0:64, u, 0:64])
            v.tensor_copy(hv[:, g, cs][64:128], ptr[0:64, u, 64:128])
    k.step("vector", deps=("tensor",), body=evt)

    def mmt3(te):
        for u, uu in enumerate(units):
            c, g = divmod(uu, 4)
            cs = ts_(c, CH)
            te.transpose(ptr[0:64, u, :], wseg[:, g, cs], k.c("ident_f32")[:])
    k.step("tensor", deps=("vector",), body=mmt3)

    def evt2(v):
        for u, uu in enumerate(units):
            c, g = divmod(uu, 4)
            cs = ts_(c, CH)
            v.tensor_copy(wseg[:, g, cs][0:64], ptr[0:64, u, 0:64])
            v.tensor_copy(wseg[:, g, cs][64:128], ptr[0:64, u, 64:128])
    k.step("vector", deps=("tensor",), body=evt2)


def _tail(k, s, seg, rows, srow, gseg, oseg, sq1):
    nc, A, w, sw = k.nc, k.A, k.w, k.smallw
    x_s = k.x_s
    onrm = A.bf16([128, 4, TS])
    d1seg = A.bf16([128, NPT, TS])
    k.o_wo = A.off
    wo_s = A.bf16([128, 4, D])
    zseg = A.bf16([128, NPT, TS])
    ytile = A.f32([128, TS])
    wff1 = A.bf16([128, NPT, 512])
    wff3 = A.bf16([128, NPT, 512])
    wff2 = A.bf16([128, 4, D])
    ubuf = A.bf16([128, NPT, TS])
    ffwr = A.f32([1, D])
    WT = k.dsem(f"wt{s}")

    with nc.psum_tensor(f"phn{s}", [HN, TS], F32) as pn:
        for g in range(4):
            k.step("scalar", deps=("scalar",) if g else ALL,
                   body=lambda sc, g=g: sc.activation(sq1[:], oseg[:, g, :], AF.Square))
            k.step("tensor", deps=("scalar",),
                   body=lambda te, g=g: te.matmul(pn[:, :], k.c(f"esel{g}")[:], sq1[:],
                                                  start=(g == 0), stop=(g == 3)))

        def vmean(v):
            v.tensor_scalar(srow[0:HN, 4, :], pn[:, :], 1.0 / 64.0, None, op0=ALU.mult)
        k.step("vector", deps=("tensor",), body=vmean)
    k.step("scalar", deps=("vector",), body=lambda sc: sc.activation(
        srow[0:HN, 4, :], srow[0:HN, 4, :], AF.Sqrt, bias=k.c("eps_rms")[0:HN]))
    k.step("vector", deps=("scalar",), body=lambda v: v.reciprocal(
        srow[0:HN, 4, :], srow[0:HN, 4, :]))
    with nc.psum_tensor(f"phb{s}", [128, 4, TS], F32) as pb:
        def mmb(te):
            for g in range(4):
                te.matmul(pb[:, g, :], k.c(f"ebc{g}")[:], srow[0:HN, 4, :],
                          start=True, stop=True)
        k.step("tensor", deps=("vector",), body=mmb)

        def vnrm(v):
            for g in range(4):
                v.scalar_tensor_tensor(sq1[:], oseg[:, g, :], sw["hnw_pg"][:, g:g + 1],
                                       pb[:, g, :], op0=ALU.mult, op1=ALU.mult)
                v.drain()
                v.tensor_copy(onrm[:, g, :], sq1[:])
                v.drain()
        k.step("vector", deps=("tensor",), body=vnrm)

    def ldwo(g):
        g.dma_start(out=wo_s, in_=w["Wo"].rearrange("(i p) f -> p i f", p=128)).then_inc(WT[0], 16)
        WT[1] += 16
    k.step("gpsimd", deps=("vector", "scalar", "tensor"), body=ldwo)
    with nc.psum_tensor(f"pwo{s}", [128, 2, TS], F32) as po:
        t_ev = {}
        for i in range(NPT):
            def mmwo(te, i=i):
                for kt in range(4):
                    te.matmul(po[:, i % 2, :], wo_s[:, kt, ts_(i, 128)], onrm[:, kt, :],
                              start=(kt == 0), stop=(kt == 3))
            k.step("tensor", deps=("vector",) if i == 0 else (),
                   deps_at=[("vector", t_ev.get(i - 2, 0))],
                   dma=[(WT, WT[1])] if i == 0 else (), body=mmwo)
            t_mm = k.tick["tensor"][1]

            def evwo(v, i=i):
                v.tensor_tensor(d1seg[:, i, :], po[:, i % 2, :], gseg[:, i, :], op=ALU.mult)
            k.step("vector", deps_at=[("tensor", t_mm)], body=evwo)
            t_ev[i] = k.tick["vector"][1]

    def ldff(sy):
        sy.dma_start(out=ffwr, in_=w["ff_norm_w"]).then_inc(WT[0], 16)
        WT[1] += 16
    k.step("sync", deps=("vector", "scalar", "tensor"), body=ldff)
    with nc.psum_tensor(f"pz{s}", [128, 3, TS], F32) as pz:
        for i in range(NPT):
            def bc1(te, i=i):
                te.matmul(pz[:, 0, :], k.c("ones_row")[:], rows[0:1, 0, :],
                          start=True, stop=True)
                te.matmul(pz[:, 1, :], k.c("ones_row")[:], rows[0:1, 1, :],
                          start=True, stop=True)
            k.step("tensor", deps=("vector", "scalar"), body=bc1)

            def vy(v, i=i):
                v.tensor_tensor(ytile[:], x_s[:, i, seg], pz[:, 0, :], op=ALU.mult)
                v.drain()
                v.tensor_tensor(ytile[:], ytile[:], pz[:, 1, :], op=ALU.add)
                v.drain()
                v.tensor_tensor(ytile[:], ytile[:], d1seg[:, i, :], op=ALU.add)
            k.step("vector", deps=("tensor",), body=vy)
            k.step("scalar", deps=("vector",),
                   body=lambda sc, i=i: sc.activation(sq1[:], ytile[:], AF.Square))
            k.step("tensor", deps=("scalar",),
                   body=lambda te, i=i: te.matmul(pz[:, 2, :][0:1, :], k.c("ones_col")[:],
                                                  sq1[:], start=(i == 0), stop=(i == NPT - 1)))

        def vrms(v):
            v.tensor_scalar(rows[0:1, 4, :], pz[:, 2, :][0:1, :], 1.0 / D, None, op0=ALU.mult)
        k.step("vector", deps=("tensor",), body=vrms)
        k.step("scalar", deps=("vector",), body=lambda sc: sc.activation(
            rows[0:1, 4, :], rows[0:1, 4, :], AF.Sqrt, bias=k.c("eps_rms")[0:1]))
        k.step("vector", deps=("scalar",), body=lambda v: v.reciprocal(
            rows[0:1, 4, :], rows[0:1, 4, :]))

        for i in range(NPT):
            def bc2(te, i=i):
                te.matmul(pz[:, 0, :], k.c("ones_row")[:], rows[0:1, 0, :],
                          start=True, stop=True)
                te.matmul(pz[:, 1, :], k.c("ones_row")[:], rows[0:1, 1, :],
                          start=True, stop=True)
                te.matmul(pz[:, 2, :], ffwr[0:1, ts_(i, 128)], rows[0:1, 4, :],
                          start=True, stop=True)
            k.step("tensor", deps=("vector", "scalar"),
                   dma=[(WT, WT[1])] if i == 0 else (), body=bc2)

            def vz2(v, i=i):
                v.tensor_tensor(ytile[:], x_s[:, i, seg], pz[:, 0, :], op=ALU.mult)
                v.drain()
                v.tensor_tensor(ytile[:], ytile[:], pz[:, 1, :], op=ALU.add)
                v.drain()
                v.tensor_tensor(ytile[:], ytile[:], d1seg[:, i, :], op=ALU.add)
                v.drain()
                v.tensor_tensor(zseg[:, i, :], ytile[:], pz[:, 2, :], op=ALU.mult)
            k.step("vector", deps=("tensor",), body=vz2)

    with nc.psum_tensor(f"pfin{s}", [128, 2, TS], F32) as pq:
        for i in range(NPT):
            def bc3(te, i=i):
                te.matmul(pq[:, 0, :], k.c("ones_row")[:], rows[0:1, 0, :],
                          start=True, stop=True)
                te.matmul(pq[:, 1, :], k.c("ones_row")[:], rows[0:1, 1, :],
                          start=True, stop=True)
            k.step("tensor", deps=("vector", "scalar"), body=bc3)

            def vfin(v, i=i):
                v.tensor_tensor(ytile[:], x_s[:, i, seg], pq[:, 0, :], op=ALU.mult)
                v.drain()
                v.tensor_tensor(ytile[:], ytile[:], pq[:, 1, :], op=ALU.add)
                v.drain()
                v.tensor_tensor(ytile[:], ytile[:], d1seg[:, i, :], op=ALU.add)
                v.drain()
                v.tensor_tensor(x_s[:, i, seg], x_s[:, i, seg], ytile[:], op=ALU.add)
            k.step("vector", deps=("tensor",), body=vfin)

    FB = DFF // 512
    wset = [(wff1, wff3, wff2),
            (A._view(k.o_wo, [128, NPT, 512], BF16),
             A._view(k.o_gseg, [128, NPT, 512], BF16),
             A._view(k.o_oseg, [128, 4, D], BF16))]
    needs = {}

    def emit_ldw(fb, dep_tick, first=False):
        if fb >= FB:
            return
        wb = fb % 2
        w1b, w3b, w2b = wset[wb]

        def ldw(g, fb=fb):
            g.dma_start(out=w1b, in_=w["Wff1"][:, ts_(fb, 512)].rearrange(
                "(i p) f -> p i f", p=128)).then_inc(WT[0], 16)
            g.dma_start(out=w3b, in_=w["Wff3"][:, ts_(fb, 512)].rearrange(
                "(i p) f -> p i f", p=128)).then_inc(WT[0], 16)
            g.dma_start(out=w2b, in_=w["Wff2"][ts_(fb, 512), :].rearrange(
                "(i p) f -> p i f", p=128)).then_inc(WT[0], 16)
            WT[1] += 48
        if first or dep_tick == 0:
            k.step("gpsimd", deps=("tensor", "vector", "scalar"), body=ldw)
        else:
            k.step("gpsimd", deps_at=[("tensor", dep_tick)], body=ldw)
        needs[fb] = WT[1]

    emit_ldw(0, 0, first=True)
    t_evb = 0
    t_vacc = 0
    t_w2f = {}
    with nc.psum_tensor(f"pf{s}", [128, 8, TS], F32) as pf:
        for fb in range(FB):
            p = fb % 2
            HA = pf[:, 4 * p:4 * p + 4, :]
            HB = pf[:, 4 - 4 * p:8 - 4 * p, :]
            uA = ubuf[:, 0:4, :] if p == 0 else ubuf[:, 4:8, :]
            uU = ubuf[:, 4:8, :] if p == 0 else ubuf[:, 0:4, :]
            w1b, w3b, w2b = wset[fb % 2]
            if fb + 1 < FB:
                emit_ldw(fb + 1, t_w2f.get(fb - 1, 0))

            def mmf(te, w1b=w1b, HA=HA):
                for m in range(4):
                    for i in range(NPT):
                        te.matmul(HA[:, m, :], w1b[:, i, ts_(m, 128)], zseg[:, i, :],
                                  start=(i == 0), stop=(i == NPT - 1))
            k.step("tensor", deps=("vector", "scalar") if fb == 0 else (),
                   deps_at=[("vector", t_evb)], dma=[(WT, needs[fb])], body=mmf)
            t_mmf = k.tick["tensor"][1]

            def evf(sc, HA=HA, uA=uA):
                for m in range(4):
                    sc.activation(uA[:, m, :], HA[:, m, :], AF.Silu,
                                  bias=k.c("zero_col")[:])
            k.step("scalar", deps_at=[("tensor", t_mmf), ("vector", t_evb)], body=evf)
            t_evf = k.tick["scalar"][1]

            def mmf3(te, w3b=w3b, HB=HB):
                for m in range(4):
                    for i in range(NPT):
                        te.matmul(HB[:, m, :], w3b[:, i, ts_(m, 128)], zseg[:, i, :],
                                  start=(i == 0), stop=(i == NPT - 1))
            k.step("tensor", deps_at=[("vector", t_vacc)], body=mmf3)
            t_mmf3 = k.tick["tensor"][1]

            def evf3(v, HB=HB, uA=uA, uU=uU):
                for m in range(4):
                    v.tensor_tensor(uU[:, m, :], HB[:, m, :], uA[:, m, :], op=ALU.mult)
            k.step("vector", deps_at=[("tensor", t_mmf3), ("scalar", t_evf)], body=evf3)
            t_evb = k.tick["vector"][1]

            for dh in range(2):
                def mmw2(te, dh=dh, HA=HA, uU=uU, w2b=w2b):
                    for m in range(4):
                        for kt in range(4):
                            te.matmul(HA[:, m, :], w2b[:, kt, ts_(dh * 4 + m, 128)],
                                      uU[:, kt, :], start=(kt == 0), stop=(kt == 3))
                k.step("tensor", deps_at=[("vector", max(t_evb, t_vacc)),
                                          ("scalar", t_evf)], body=mmw2)
                t_w2 = k.tick["tensor"][1]

                def vadd(v, dh=dh, HA=HA):
                    for m in range(4):
                        ii = dh * 4 + m
                        v.tensor_tensor(x_s[:, ii, seg], x_s[:, ii, seg],
                                        HA[:, m, :], op=ALU.add)
                k.step("vector", deps_at=[("tensor", t_w2)], body=vadd)
                t_vacc = k.tick["vector"][1]
            t_w2f[fb] = k.tick["tensor"][1]


def _ffn_block(k):
    nc, A, w, x_s = k.nc, k.A, k.w, k.x_s
    ROWS = k.ROWS
    A.push("ffn")
    o_nx = A.off
    nx = A.bf16([128, NPT, T])
    X2b = A._view(o_nx, [128, NPT, T], BF16)
    xf = X2b
    o_qf = A.off
    QF = A.bf16([128, NPT, T])
    o_kc = A.off
    Kc = A.bf16([128, 2, T])
    o_vct = A.off
    VcT = A.bf16([128, 16, 256])
    o_ob = A.off
    Obuf = A.bf16([128, NPT, 512])
    Vc = A._view(o_ob, [128, 2, T], BF16)
    escr = A._view(o_ob, [128, 2048], BF16)
    o_wsb = A.off
    wsb = A.bf16([128, NPT, 512])
    sq1 = A.f32([128, 512])
    mlcols = A.f32([128, 64])
    o_pt = A.off
    Pt = A.bf16([128, 2, 512])
    rbc = A.f32([128, 512])
    gr = A.f32([1, D])
    mrow = ROWS.rearrange("a r t -> a (r t)")[0:1, 0:2048]
    lrow = ROWS.rearrange("a r t -> a (r t)")[0:1, 2048:4096]
    WL = k.dsem("fwl")
    GL = k.dsem("fgl")

    def ldg1(sy):
        sy.dma_start(out=gr, in_=w["g1"]).then_inc(GL[0], 16)
        GL[1] += 16
    k.step("sync", body=ldg1)
    for tg in range(4):
        sg = ts_(tg, 512)
        with nc.psum_tensor(f"fps{tg}", [1, 2, 512], F32) as psA:
            _ln_rows(k, ROWS, lambda i: x_s[:, i, sg], 512, "eps_ln", True, sq1, psA)
        with nc.psum_tensor(f"fpn{tg}", [128, 2, 512], F32) as pb:
            for i in range(NPT):
                def bc(te, i=i, tg=tg):
                    te.matmul(pb[:, 0, :], gr[0:1, ts_(i, 128)], ROWS[0:1, 2, :],
                              start=True, stop=True)
                    te.matmul(pb[:, 1, :], gr[0:1, ts_(i, 128)], ROWS[0:1, 3, :],
                              start=True, stop=True)
                k.step("tensor", deps=("vector", "scalar"),
                       dma=[(GL, GL[1])] if (tg == 0 and i == 0) else (), body=bc)

                def vnx(v, i=i, sg=sg):
                    v.tensor_tensor(sq1[:], x_s[:, i, sg], pb[:, 0, :], op=ALU.mult)
                    v.drain()
                    v.tensor_tensor(nx[:, i, sg], sq1[:], pb[:, 1, :], op=ALU.add)
                k.step("vector", deps=("tensor",), body=vnx)

    if "nx0" in k.want_dbg:
        k.debug_out("nx0", lambda: nx[:, 0, :])

    with nc.psum_tensor("fpq", [128, 8, 512], F32) as pp:
        t_evq = {}
        jq = 0
        for half in range(2):
            def ldq(g, half=half):
                g.dma_start(out=wsb, in_=w["fWqT"][:, ts_(half, 512)].rearrange(
                    "(i p) f -> p i f", p=128)).then_inc(WL[0], 16)
                WL[1] += 16
            k.step("gpsimd", deps=("tensor",), body=ldq)
            need = WL[1]
            for tg in range(4):
                hp = jq % 2
                def mmq(te, half=half, tg=tg, hp=hp):
                    for m in range(4):
                        for i in range(NPT):
                            te.matmul(pp[:, 4 * hp + m, :], wsb[:, i, ts_(m, 128)],
                                      nx[:, i, ts_(tg, 512)],
                                      start=(i == 0), stop=(i == NPT - 1))
                k.step("tensor", deps=("scalar", "vector") if jq == 0 else (),
                       deps_at=[("scalar", t_evq.get(jq - 2, 0))],
                       dma=[(WL, need)], body=mmq)
                t_mmq = k.tick["tensor"][1]

                def evq(sc, half=half, tg=tg, hp=hp):
                    for m in range(4):
                        sc.activation(QF[:, half * 4 + m, ts_(tg, 512)],
                                      pp[:, 4 * hp + m, :],
                                      AF.Identity, bias=k.c("zero_col")[:])
                k.step("scalar", deps_at=[("tensor", t_mmq)], body=evq)
                t_evq[jq] = k.tick["scalar"][1]
                jq += 1

        kvw = wsb.rearrange("p i f -> p (i f)")[:, 0:2048].rearrange(
            "p (i c) -> p i c", i=NPT, c=256)
        for wn, dst in [("fWkcT", Kc), ("fWvcT", Vc)]:
            def ldkc(g, wn=wn):
                g.dma_start(out=kvw, in_=w[wn].rearrange("(i p) c -> p i c", p=128)
                            ).then_inc(WL[0], 16)
                WL[1] += 16
            k.step("gpsimd", deps=("tensor",), body=ldkc)
            need = WL[1]
            for tg in range(4):
                def mmk(te, tg=tg):
                    for m in range(2):
                        for i in range(NPT):
                            te.matmul(pp[:, m, :], kvw[:, i, ts_(m, 128)],
                                      nx[:, i, ts_(tg, 512)],
                                      start=(i == 0), stop=(i == NPT - 1))
                k.step("tensor", deps=("scalar", "vector"), dma=[(WL, need)], body=mmk)

                def evk(sc, dst=dst, tg=tg):
                    for m in range(2):
                        sc.activation(dst[:, m, ts_(tg, 512)], pp[:, m, :],
                                      AF.Identity, bias=k.c("zero_col")[:])
                k.step("scalar", deps=("tensor",), body=evk)

    with nc.psum_tensor("fvt", [128, 4, 128], BF16) as pt:
        for rnd in range(8):
            def mmt(te, rnd=rnd):
                for j in range(4):
                    u = rnd * 4 + j
                    tt_i, ct = divmod(u, 2)
                    te.transpose(pt[:, j, :], Vc[:, ct, ts_(tt_i, 128)],
                                 k.c("ident_bf")[:])
            k.step("tensor", deps=("scalar", "vector"), body=mmt)

            def evt(v, rnd=rnd):
                for j in range(4):
                    u = rnd * 4 + j
                    tt_i, ct = divmod(u, 2)
                    v.tensor_copy(VcT[:, tt_i, ts_(ct, 128)], pt[:, j, :])
            k.step("vector", deps=("tensor",), body=evt)

    if "QF0" in k.want_dbg:
        k.debug_out("QF0", lambda: QF[:, 0, :])
        k.debug_out("Kc0", lambda: Kc[:, 0, :])
        k.debug_out("VcT0", lambda: VcT[:, 0, :])

    for qw in range(4):
        qs = ts_(qw, 512)
        with nc.psum_tensor(f"fst{qw}", [128, 2, 2048], F32) as ps2:
            t_sexp = {}
            t_vsum = 0
            for h in range(4):
                for qt in range(4):
                    col = h * 4 + qt
                    qtile = qw * 4 + qt
                    hp = col % 2
                    ps = ps2[:, hp, :]

                    def mms(te, h=h, qtile=qtile, ps=ps):
                        for kg in range(4):
                            for kt in range(2):
                                te.matmul(ps[:, ts_(kg, 512)],
                                          QF[:, 2 * h + kt, ts_(qtile, 128)],
                                          Kc[:, kt, ts_(kg, 512)],
                                          start=(kt == 0), stop=(kt == 1))
                    k.step("tensor", deps=("vector", "scalar") if col == 0 else (),
                           deps_at=[("scalar", t_sexp.get(col - 2, 0))], body=mms)
                    t_mms = k.tick["tensor"][1]

                    def vms(v, col=col, ps=ps):
                        v.reduce_max(mlcols[:, col:col + 1], ps[:], axis=AX)
                        v.drain()
                        v.tensor_scalar(mlcols[:, col:col + 1], mlcols[:, col:col + 1],
                                        -1.0, None, op0=ALU.mult)
                        v.drain()
                        v.tensor_scalar(mlcols[:, 16 + col:17 + col],
                                        mlcols[:, col:col + 1], S16, None, op0=ALU.mult)
                    k.step("vector", deps_at=[("tensor", t_mms)], body=vms)
                    t_vms = k.tick["vector"][1]

                    def sexp(sc, col=col, ps=ps):
                        for kg in range(4):
                            sc.activation(escr[:, ts_(kg, 512)], ps[:, ts_(kg, 512)],
                                          AF.Exp, scale=S16,
                                          bias=mlcols[:, 16 + col:17 + col],
                                          accum_out=mlcols[:, 48 + 4 * hp + kg:49 + 4 * hp + kg]
                                          if False else mlcols[:, 48 + kg:49 + kg])
                    k.step("scalar", deps_at=[("vector", k.tick["vector"][1])], body=sexp)
                    t_sexp[col] = k.tick["scalar"][1]

                    def vsum(v, col=col):
                        v.tensor_tensor(mlcols[:, 56:57], mlcols[:, 48:49],
                                        mlcols[:, 49:50], op=ALU.add)
                        v.tensor_tensor(mlcols[:, 57:58], mlcols[:, 50:51],
                                        mlcols[:, 51:52], op=ALU.add)
                        v.drain()
                        v.tensor_tensor(mlcols[:, 32 + col:33 + col], mlcols[:, 56:57],
                                        mlcols[:, 57:58], op=ALU.add)
                    k.step("vector", deps_at=[("scalar", t_sexp[col])], body=vsum)

        def vrecl(v):
            v.reciprocal(mlcols[:, 32:48], mlcols[:, 32:48])
        k.step("vector", deps=("scalar",), body=vrecl)

        with nc.psum_tensor(f"fmr{qw}", [1, 2, 2048], F32) as pmr:
            def mmrow(te):
                for col in range(16):
                    te.transpose(pmr[0:1, 0, ts_(col, 128)],
                                 mlcols[:, col:col + 1], k.c("ident_f32")[:])
                    te.transpose(pmr[0:1, 1, ts_(col, 128)],
                                 mlcols[:, 32 + col:33 + col], k.c("ident_f32")[:])
            k.step("tensor", deps=("vector",), body=mmrow)

            def evrow(v):
                v.tensor_copy(mrow, pmr[0:1, 0, :])
                v.tensor_copy(lrow, pmr[0:1, 1, :])
            k.step("vector", deps=("tensor",), body=evrow)

        Pt4 = A._view(o_wsb, [128, 4, 512], BF16)
        t_vn = 0
        with nc.psum_tensor(f"fpv{qw}", [128, 8, 512], F32) as pv:
            for h in range(4):
                t_sst = {}
                t_mmst = {}
                t_mmpv = {}
                for kr in range(9):
                    if kr < 8:
                        def mmst(te, h=h, kr=kr):
                            sp = 2 * (kr % 2)
                            for j in range(2):
                                kvt = kr * 2 + j
                                for kt in range(2):
                                    te.matmul(pv[:, sp + j, :], Kc[:, kt, ts_(kvt, 128)],
                                              QF[:, 2 * h + kt, qs],
                                              start=(kt == 0), stop=False)
                                te.matmul(pv[:, sp + j, :], k.c("ones_row")[:],
                                          mrow[0:1, ts_(h, 512)], start=False, stop=True)
                        k.step("tensor",
                               deps=("vector", "scalar") if (h == 0 and kr == 0) else (),
                               deps_at=[("scalar", t_sst.get(kr - 2, 0))], body=mmst)
                        t_mmst[kr] = k.tick["tensor"][1]
                    if kr >= 1:
                        def mmpv(te, h=h, kr=kr):
                            kp = kr - 1
                            sp = 2 * (kp % 2)
                            for j in range(2):
                                kvt = kp * 2 + j
                                for ct in range(2):
                                    te.matmul(pv[:, 4 + ct, :], VcT[:, kvt, ts_(ct, 128)],
                                              Pt4[:, sp + j, :],
                                              start=(kp == 0 and j == 0),
                                              stop=(kp == 7 and j == 1))
                        k.step("tensor", deps_at=[("scalar", t_sst[kr - 1]),
                                                  ("vector", t_vn)], body=mmpv)
                        t_mmpv[kr - 1] = k.tick["tensor"][1]
                    if kr < 8:
                        def sst_(sc, kr=kr):
                            sp = 2 * (kr % 2)
                            for j in range(2):
                                sc.activation(Pt4[:, sp + j, :], pv[:, sp + j, :],
                                              AF.Exp, scale=S16, bias=k.c("zero_col")[:])
                        k.step("scalar", deps_at=[("tensor", t_mmst[kr])], body=sst_)
                        t_sst[kr] = k.tick["scalar"][1]

                def mmrb(te, h=h):
                    te.matmul(pv[:, 6, :], k.c("ones_row")[:], lrow[0:1, ts_(h, 512)],
                              start=True, stop=True)
                k.step("tensor", deps_at=[("scalar", k.tick["scalar"][1])], body=mmrb)
                t_rb = k.tick["tensor"][1]
                k.step("scalar", deps_at=[("tensor", t_rb)], body=lambda sc: sc.activation(
                    rbc[:], pv[:, 6, :], AF.Identity, bias=k.c("zero_col")[:]))
                t_rbe = k.tick["scalar"][1]

                def vnorm(v, h=h):
                    for ct in range(2):
                        v.tensor_tensor(Obuf[:, 2 * h + ct, :], pv[:, 4 + ct, :],
                                        rbc[:], op=ALU.mult)
                k.step("vector", deps_at=[("scalar", t_rbe),
                                          ("tensor", k.tick["tensor"][1])], body=vnorm)
                t_vn = k.tick["vector"][1]

        if qw == 0 and "Ob0" in k.want_dbg:
            k.debug_out("Ob0", lambda: Obuf[:, 0, :])
        with nc.psum_tensor(f"fl2{qw}", [1, 2, 512], F32) as psA:
            _ln_rows(k, ROWS, lambda i: x_s[:, i, qs], 512, "eps_ln", False, sq1, psA)

        with nc.psum_tensor(f"fwo{qw}", [128, 4, 512], F32) as po, \
             nc.psum_tensor(f"fwo2{qw}", [128, 2, 512], F32) as pb2, \
             nc.psum_tensor(f"fws{qw}", [1, 512], F32) as psq:
            for half in range(2):
                def ldwo(g, half=half):
                    g.dma_start(out=wsb, in_=w["fWoT"][:, ts_(half, 512)].rearrange(
                        "(i p) f -> p i f", p=128)).then_inc(WL[0], 16)
                    WL[1] += 16
                k.step("gpsimd", deps=("tensor",), body=ldwo)
                need = WL[1]

                def mmwo(te, half=half):
                    for m in range(4):
                        for i in range(NPT):
                            te.matmul(po[:, m, :], wsb[:, i, ts_(m, 128)], Obuf[:, i, :],
                                      start=(i == 0), stop=(i == NPT - 1))
                k.step("tensor", deps=("scalar", "vector"), dma=[(WL, need)], body=mmwo)

                for m in range(4):
                    ii = half * 4 + m

                    def bcl(te, ii=ii):
                        te.matmul(pb2[:, 0, :], k.c("ones_row")[:], ROWS[0:1, 0, :],
                                  start=True, stop=True)
                        te.matmul(pb2[:, 1, :], k.c("ones_row")[:], ROWS[0:1, 1, :],
                                  start=True, stop=True)
                    k.step("tensor", deps=("vector", "scalar"), body=bcl)

                    def vx2(v, ii=ii, m=m):
                        v.tensor_tensor(sq1[:], x_s[:, ii, qs], pb2[:, 0, :], op=ALU.mult)
                        v.drain()
                        v.tensor_tensor(sq1[:], sq1[:], pb2[:, 1, :], op=ALU.add)
                        v.drain()
                        v.tensor_tensor(sq1[:], sq1[:], po[:, m, :], op=ALU.add)
                        v.drain()
                        v.tensor_copy(X2b[:, ii, qs], sq1[:])
                        v.drain()
                        v.tensor_tensor(x_s[:, ii, qs], x_s[:, ii, qs], sq1[:], op=ALU.add)
                    k.step("vector", deps=("tensor",), body=vx2)
                    k.step("scalar", deps=("vector",),
                           body=lambda sc, ii=ii: sc.activation(
                               sq1[:], X2b[:, ii, qs], AF.Square))
                    k.step("tensor", deps=("scalar",),
                           body=lambda te, ii=ii: te.matmul(
                               psq[:, :], k.c("ones_col")[:], sq1[:],
                               start=(ii == 0), stop=(ii == NPT - 1)))

            def vrms(v):
                v.tensor_scalar(ROWS[0:1, 4, :], psq[0:1, :], 1.0 / D, None, op0=ALU.mult)
            k.step("vector", deps=("tensor",), body=vrms)
            k.step("scalar", deps=("vector",), body=lambda sc: sc.activation(
                ROWS[0:1, 4, :], ROWS[0:1, 4, :], AF.Sqrt, bias=k.c("eps_rms")[0:1]))
            k.step("vector", deps=("scalar",), body=lambda v: v.reciprocal(
                ROWS[0:1, 4, :], ROWS[0:1, 4, :]))

        GL2 = k.dsem("fg2")
        if qw == 0:
            def ldg2(sy):
                sy.dma_start(out=gr, in_=w["g2"]).then_inc(GL2[0], 16)
                GL2[1] += 16
            k.step("sync", deps=("tensor",), body=ldg2)
        with nc.psum_tensor(f"fxf{qw}", [128, 2, 512], F32) as pxf:
            for i in range(NPT):
                def bxf(te, i=i):
                    te.matmul(pxf[:, i % 2, :], gr[0:1, ts_(i, 128)], ROWS[0:1, 4, :],
                              start=True, stop=True)
                k.step("tensor", deps=("vector", "scalar"),
                       dma=[(GL2, GL2[1])] if i == 0 else (), body=bxf)

                def vxf(v, i=i):
                    v.tensor_tensor(X2b[:, i, qs], X2b[:, i, qs], pxf[:, i % 2, :],
                                    op=ALU.mult)
                k.step("vector", deps=("tensor",), body=vxf)

    if "X2b0" in k.want_dbg:
        k.debug_out("X2b0", lambda: X2b[:, 0, :])

    crow = A._view(o_kc, [8, T], F32)
    ltg = A._view(o_vct, [8, 512], F32)
    ttm = A._view(o_vct + 512, [128, 16, 8], F32)
    tsc = A._view(o_vct + 640, [128, 16], F32)
    wrb = A._view(o_vct + 656, [128, 8, 8], BF16)
    m1m = A._view(o_vct + 688, [128, 16, 8], F32)
    m2m = A._view(o_vct + 816, [128, 16, 8], F32)
    wcol = A._view(o_vct + 944, [128, 16], F32)
    w2col = A._view(o_vct + 960, [128, 16], F32)
    zcol = A._view(o_vct + 976, [128, 16], F32)
    negm = A._view(o_vct + 992, [128, 16], F32)
    cbc = A._view(o_vct + 1008, [128, 512], F32)
    WR = k.dsem("fwr")

    def ldwr(g):
        g.dma_start(out=wrb, in_=w["fWrT"].rearrange("(i p) e -> p i e", p=128)
                    ).then_inc(WR[0], 16)
        WR[1] += 16
    k.step("gpsimd", deps=("tensor",), body=ldwr)
    for tg in range(4):
        sg = ts_(tg, 512)
        with nc.psum_tensor(f"frt{tg}", [8, 512], F32) as pr, \
             nc.psum_tensor(f"frt2{tg}", [128, 4, 8], F32) as pt2:
            def mmr(te, tg=tg, sg=sg):
                for i in range(NPT):
                    te.matmul(pr[:, :], wrb[:, i, :], xf[:, i, sg],
                              start=(i == 0), stop=(i == NPT - 1))
            k.step("tensor", deps=("vector", "scalar"),
                   dma=[(WR, WR[1])] if tg == 0 else (), body=mmr)
            k.step("scalar", deps=("tensor",), body=lambda sc: sc.activation(
                ltg[:], pr[:, :], AF.Identity, bias=k.smallw["ebias"][:]))

            def mmtr(te, tg=tg):
                for j in range(4):
                    te.transpose(pt2[:, j, :], ltg[:, ts_(j, 128)],
                                 k.c("ident_f32")[0:8, 0:8])
            k.step("tensor", deps=("scalar",), body=mmtr)

            def vtop(v, tg=tg):
                for j in range(4):
                    v.tensor_copy(ttm[:, tg * 4 + j, :], pt2[:, j, :])
            k.step("vector", deps=("tensor",), body=vtop)

    def vt1(v):
        for j in range(16):
            v.reduce_max(tsc[:, j:j + 1], ttm[:, j, :], axis=AX)
    k.step("vector", deps=("vector", "tensor"), body=vt1)

    def vt2(v):
        for j in range(16):
            v.tensor_scalar(m1m[:, j, :], ttm[:, j, :], tsc[:, j:j + 1], None, op0=ALU.is_ge)
        v.drain()
        v.tensor_scalar(negm[:], tsc[:], -1.0, None, op0=ALU.mult)
        v.drain()
        for j in range(16):
            v.scalar_tensor_tensor(m2m[:, j, :], m1m[:, j, :], -1e30, ttm[:, j, :],
                                   op0=ALU.mult, op1=ALU.add)
        v.drain()
        for j in range(16):
            v.reduce_max(tsc[:, j:j + 1], m2m[:, j, :], axis=AX)
    k.step("vector", deps=("vector",), body=vt2)

    def st3(sc):
        for j in range(16):
            sc.activation(ttm[:, j, :], ttm[:, j, :], AF.Exp, bias=negm[:, j:j + 1])
    k.step("scalar", deps=("vector",), body=st3)

    def vt4(v):
        for j in range(16):
            v.reduce_sum(zcol[:, j:j + 1], ttm[:, j, :], axis=AX)
        v.drain()
        v.reciprocal(zcol[:], zcol[:])
        v.drain()
        for j in range(16):
            v.tensor_scalar(m2m[:, j, :], m2m[:, j, :], tsc[:, j:j + 1], None, op0=ALU.is_ge)
        v.drain()
        v.tensor_tensor(wcol[:], tsc[:], negm[:], op=ALU.add)
    k.step("vector", deps=("scalar",), body=vt4)
    k.step("scalar", deps=("vector",), body=lambda sc: sc.activation(
        wcol[:], wcol[:], AF.Exp))

    def vt5(v):
        v.tensor_tensor(wcol[:], wcol[:], zcol[:], op=ALU.mult)
        v.drain()
        v.tensor_tensor(wcol[:], zcol[:], wcol[:], op=ALU.subtract)
    k.step("vector", deps=("scalar",), body=vt5)
    k.step("scalar", deps=("vector",), body=lambda sc: sc.activation(
        wcol[:], wcol[:], AF.Sigmoid))

    def vt6(v):
        v.tensor_scalar(w2col[:], wcol[:], -1.0, 1.0, op0=ALU.mult, op1=ALU.add)
        v.drain()
        for j in range(16):
            v.tensor_scalar(m1m[:, j, :], m1m[:, j, :], wcol[:, j:j + 1], None, op0=ALU.mult)
        v.drain()
        for j in range(16):
            v.scalar_tensor_tensor(ttm[:, j, :], m2m[:, j, :], w2col[:, j:j + 1],
                                   m1m[:, j, :], op0=ALU.mult, op1=ALU.add)
    k.step("vector", deps=("scalar",), body=vt6)

    with nc.psum_tensor("fct", [8, 4, 128], F32) as pc:
        for rnd in range(4):
            def mmtc(te, rnd=rnd):
                for j in range(4):
                    u = rnd * 4 + j
                    te.transpose(pc[:, j, :], ttm[:, u, :], k.c("ident_f32")[:])
            k.step("tensor", deps=("vector",), body=mmtc)

            def evc(v, rnd=rnd):
                for j in range(4):
                    u = rnd * 4 + j
                    v.tensor_copy(crow[0:8, ts_(u, 128)], pc[:, j, :])
            k.step("vector", deps=("tensor",), body=evc)

    if "crow0" in k.want_dbg:
        k.debug_out("crow0", lambda: crow[0:8, :])

    w1h = [A._view(o_qf, [128, NPT, 512], BF16),
           A._view(o_ob, [128, NPT, 512], BF16)]
    w3h = [A._view(o_qf + 2048, [128, NPT, 512], BF16),
           A._view(o_wsb, [128, NPT, 512], BF16)]
    w2h = [A._view(o_qf + 4096, [128, 4, D], BF16),
           A._view(o_pt, [128, 4, D], BF16)]
    ubuf = A._view(o_qf + 6144, [128, NPT, 512], BF16)
    TGM = 512
    EFH = [(e, fh) for e in range(9) for fh in range(2)]

    def emit_ldew(idx, dep_tick):
        if idx >= len(EFH):
            return
        e, fh = EFH[idx]
        shared = (e == 8)
        wb = idx % 2

        def ldew(g):
            if shared:
                s1, s3, s2 = w["sW1T"], w["sW3T"], w["sW2T"]
            else:
                s1, s3, s2 = w["eW1T"][e], w["eW3T"][e], w["eW2T"][e]
            g.dma_start(out=w1h[wb], in_=s1[:, ts_(fh, 512)].rearrange(
                "(i p) f -> p i f", p=128)).then_inc(WL[0], 16)
            g.dma_start(out=w3h[wb], in_=s3[:, ts_(fh, 512)].rearrange(
                "(i p) f -> p i f", p=128)).then_inc(WL[0], 16)
            g.dma_start(out=w2h[wb], in_=s2[ts_(fh, 512), :].rearrange(
                "(i p) f -> p i f", p=128)).then_inc(WL[0], 16)
            WL[1] += 48
        k.step("gpsimd", deps_at=[("tensor", dep_tick)], body=ldew)
        return WL[1]

    with nc.psum_tensor("fmoe", [128, 8, TGM], F32) as pa:
        SA = pa[:, 0:4, :]
        SB = pa[:, 4:8, :]
        t_vacc0 = 0
        t_vacc1 = 0
        t_evb23 = 0
        tick_w2 = {}
        needs = {}
        needs[0] = emit_ldew(0, 0)
        j = 0
        for idx, (e, fh) in enumerate(EFH):
            shared = (e == 8)
            for tg in range(4):
                sg = ts_(tg, TGM)
                p = j % 2
                uA = ubuf[:, 0:4, :] if p == 0 else ubuf[:, 4:8, :]
                uU = ubuf[:, 4:8, :] if p == 0 else ubuf[:, 0:4, :]
                dma = [(WL, needs[idx])] if tg == 0 else ()

                def mma(te, sg=sg, wb=idx % 2):
                    for ft in range(4):
                        for i in range(NPT):
                            te.matmul(SA[:, ft, :], w1h[wb][:, i, ts_(ft, 128)],
                                      xf[:, i, sg], start=(i == 0),
                                      stop=(i == NPT - 1))
                k.step("tensor", deps=("vector", "scalar") if j == 0 else (),
                       deps_at=[("vector", t_vacc0)], dma=dma, body=mma)
                t_mma = k.tick["tensor"][1]

                def eva01(sc, uA=uA):
                    for ft in (0, 1):
                        sc.activation(uA[:, ft, :], SA[:, ft, :], AF.Silu,
                                      bias=k.c("zero_col")[:])
                k.step("scalar", deps_at=[("tensor", t_mma)], body=eva01)
                t_eva01 = k.tick["scalar"][1]

                def mmb01(te, sg=sg, wb=idx % 2):
                    for ft in (0, 1):
                        for i in range(NPT):
                            te.matmul(SB[:, ft, :], w3h[wb][:, i, ts_(ft, 128)],
                                      xf[:, i, sg], start=(i == 0),
                                      stop=(i == NPT - 1))
                k.step("tensor", deps_at=[("vector", t_vacc1)], body=mmb01)
                t_mmb01 = k.tick["tensor"][1]

                if not shared:
                    def mmcb(te, e=e, sg=sg):
                        te.matmul(SA[:, 0, :], k.c(f"erow{e}")[:], crow[0:8, sg],
                                  start=True, stop=True)
                    k.step("tensor", deps_at=[("scalar", t_eva01)], body=mmcb)
                    t_mmcb = k.tick["tensor"][1]
                    k.step("scalar", deps_at=[("tensor", t_mmcb),
                                              ("vector", t_evb23)],
                           body=lambda sc: sc.activation(
                               cbc[:], SA[:, 0, :], AF.Identity,
                               bias=k.c("zero_col")[:]))
                t_cbcx = k.tick["scalar"][1]

                def mmb23(te, sg=sg, wb=idx % 2):
                    for ft in (2, 3):
                        for i in range(NPT):
                            te.matmul(SB[:, ft, :], w3h[wb][:, i, ts_(ft, 128)],
                                      xf[:, i, sg], start=(i == 0),
                                      stop=(i == NPT - 1))
                k.step("tensor", deps=(), body=mmb23)
                t_mmb23 = k.tick["tensor"][1]

                def eva23(sc, uA=uA):
                    for ft in (2, 3):
                        sc.activation(uA[:, ft, :], SA[:, ft, :], AF.Silu,
                                      bias=k.c("zero_col")[:])
                k.step("scalar", deps_at=[("tensor", t_mma)], body=eva23)
                t_eva23 = k.tick["scalar"][1]

                if tg == 0 and idx + 1 < len(EFH):
                    needs[idx + 1] = emit_ldew(idx + 1, tick_w2.get(idx - 1, 0))

                def evb(v, fts, uA=uA, uU=uU, shared=shared):
                    for ft in fts:
                        v.tensor_tensor(uU[:, ft, :], SB[:, ft, :], uA[:, ft, :],
                                        op=ALU.mult)
                    if not shared:
                        v.drain()
                        for ft in fts:
                            v.tensor_tensor(uU[:, ft, :], uU[:, ft, :], cbc[:],
                                            op=ALU.mult)
                k.step("vector", deps_at=[("tensor", t_mmb01),
                                          ("scalar", t_cbcx)],
                       body=lambda v: evb(v, (0, 1)))
                t_evb01 = k.tick["vector"][1]
                k.step("vector", deps_at=[("tensor", t_mmb23),
                                          ("scalar", t_eva23)],
                       body=lambda v: evb(v, (2, 3)))
                t_evb23 = k.tick["vector"][1]

                def mmw2(te, dh, kts, dst, wb=idx % 2):
                    for m in range(4):
                        for kt in kts:
                            te.matmul(dst[:, m, :],
                                      w2h[wb][:, kt, ts_(dh * 4 + m, 128)],
                                      uU[:, kt, :], start=(kt == kts[0] and kts[0] == 0),
                                      stop=(kt == 3 and kts[-1] == 3))
                k.step("tensor", deps_at=[("vector", t_evb01),
                                          ("scalar", t_eva23)],
                       body=lambda te: mmw2(te, 0, (0, 1), SA))
                k.step("tensor", deps_at=[("vector", t_evb23)],
                       body=lambda te: mmw2(te, 0, (2, 3), SA))
                t_w2d0 = k.tick["tensor"][1]

                def vacc(v, dh, src, sg=sg):
                    for m in range(4):
                        ii = dh * 4 + m
                        v.tensor_tensor(x_s[:, ii, sg], x_s[:, ii, sg],
                                        src[:, m, :], op=ALU.add)
                k.step("vector", deps_at=[("tensor", t_w2d0)],
                       body=lambda v: vacc(v, 0, SA))
                t_vacc0 = k.tick["vector"][1]

                k.step("tensor", deps_at=[("vector", t_evb23)],
                       body=lambda te: mmw2(te, 1, (0, 1), SB))
                k.step("tensor", deps=(), body=lambda te: mmw2(te, 1, (2, 3), SB))
                t_w2last = k.tick["tensor"][1]
                k.step("vector", deps_at=[("tensor", t_w2last)],
                       body=lambda v: vacc(v, 1, SB))
                t_vacc1 = k.tick["vector"][1]
                j += 1
            tick_w2[idx] = k.tick["tensor"][1]
    A.pop("ffn")


def host_prep(inputs):
    f32 = np.float32

    def C(a):
        return np.ascontiguousarray(np.asarray(a, f32))

    def T2(a):
        return np.ascontiguousarray(np.asarray(a, f32).T)

    ws = {}
    ws["norm_in_w"] = C(inputs["L0_attn_norm_in_w"]).reshape(1, D)
    ws["conv_w"] = C(np.asarray(inputs["L0_attn_conv_w"], f32)[:, 0, :])
    ws["conv_b"] = np.ascontiguousarray(C(inputs["L0_attn_conv_b"]).reshape(8, 128).T)
    for n in ["Wq", "Wk", "Wv", "Wo", "Wspike", "Wau", "Wad", "Wbeta",
              "Wu1", "Wu2", "Wff1", "Wff3", "Wff2"]:
        ws[n] = C(inputs["L0_attn_" + n])
    for n, shp in [("bau", (64, 1)), ("bu1", (64, 1)), ("bbeta", (HN, 1))]:
        ws[n] = C(inputs["L0_attn_" + n]).reshape(shp)
    ws["ff_norm_w"] = C(inputs["L0_attn_ff_norm_w"]).reshape(1, D)
    for n, src in [("bad_pg", "L0_attn_bad"), ("asp_pg", "L0_attn_alpha_spike"),
                   ("bsp_pg", "L0_attn_beta_spike"), ("hnw_pg", "L0_attn_head_norm_w")]:
        ws[n] = np.ascontiguousarray(C(inputs[src]).reshape(4, 128).T)
    ws["bu2_pg"] = np.ascontiguousarray(C(inputs["L0_attn_bu2"]).reshape(8, 128).T)
    ws["g1"] = C(inputs["L0_ffn_g1"]).reshape(1, D)
    ws["g2"] = C(inputs["L0_ffn_g2"]).reshape(1, D)
    ws["ebias"] = C(inputs["L0_ffn_expert_bias"]).reshape(HN, 1)
    for n, src in [("fWqT", "L0_ffn_Wq"), ("fWkcT", "L0_ffn_Wkc"),
                   ("fWvcT", "L0_ffn_Wvc"), ("fWoT", "L0_ffn_Wo"),
                   ("fWrT", "L0_ffn_Wr"), ("sW1T", "L0_ffn_sW1"),
                   ("sW2T", "L0_ffn_sW2"), ("sW3T", "L0_ffn_sW3")]:
        ws[n] = T2(inputs[src])
    for n in ["eW1", "eW2", "eW3"]:
        ws[n + "T"] = np.ascontiguousarray(
            np.asarray(inputs["L0_ffn_" + n], f32).transpose(0, 2, 1))
    return ws


def kernel(**inputs):
    ws = host_prep(inputs)
    x = np.asarray(inputs["L0_attn_x"], np.float32)
    nc = bass.Bass("TRN2", target_bir_lowering=False)
    build_program(nc)
    in_maps = []
    for b in range(B):
        m = dict(ws)
        m["xT"] = np.ascontiguousarray(x[b].T)
        in_maps.append(m)
    res = run_bass_kernel_spmd(nc, in_maps, core_ids=list(range(B)))
    out = np.stack([np.asarray(r["outT"], np.float32).T for r in res.results], 0)
    return np.ascontiguousarray(out)
